# version 01
# baseline (speedup 1.0000x reference)
import contextlib
import numpy as np
import ml_dtypes
import concourse.bass as bass
import concourse.mybir as mybir
from concourse.bass_utils import run_bass_kernel_spmd

F32 = mybir.dt.float32
BF16 = mybir.dt.bfloat16
AF = mybir.ActivationFunctionType
ALU = mybir.AluOpType
AX = mybir.AxisListType
BF = ml_dtypes.bfloat16

T = 4096
DM = 1024
DI = 2048
NST = 16
NH = 8
DH = 64
WIN = 512
HID = 2730
HB = 43
HK = 22
HPAD = HB * 64
TCH = 512
NCH = T // TCH
NEG = -1.0e30
EPS = 1e-5
QKS = 8.0

ENGS = ("sync", "tensor", "vector", "scalar", "gpsimd")


class Rec:
    def __init__(self):
        self.ops = {e: [] for e in ENGS}
        self.cnt = {}

    def reg(self, sem):
        self.cnt[id(sem)] = 0
        return sem

    def emit(self, eng, fn, inc=None, inc_k=1):
        if inc is not None:
            self.cnt[id(inc)] += inc_k

            def cl(e, fn=fn, inc=inc, inc_k=inc_k):
                fn(e).then_inc(inc, inc_k)
        else:
            def cl(e, fn=fn):
                fn(e)
        self.ops[eng].append(cl)
        return self.cnt[id(inc)] if inc is not None else None

    def wait(self, eng, sem, val=None):
        if val is None:
            val = self.cnt[id(sem)]
        if val > 0:
            self.ops[eng].append(lambda e, sem=sem, val=val: e.wait_ge(sem, val))

    def val(self, sem):
        return self.cnt[id(sem)]


class Ctx:
    pass


def host_prep(inputs):
    f32 = np.float32
    x = np.asarray(inputs["L0_attn_hidden_states"], f32)
    S = {}
    S["w_inproj"] = np.ascontiguousarray(np.asarray(inputs["L0_attn_in_proj_w"], f32).T)
    S["conv_w"] = np.ascontiguousarray(np.asarray(inputs["L0_attn_conv_w"], f32)[:, 0, :])
    S["conv_b"] = np.ascontiguousarray(np.asarray(inputs["L0_attn_conv_b"], f32)[:, None])
    wxp = np.asarray(inputs["L0_attn_x_proj_w"], f32)
    wxpp = np.zeros((DI, 128), f32)
    wxpp[:, 0:16] = wxp[64:80, :].T
    wxpp[:, 32:48] = wxp[80:96, :].T
    wxpp[:, 64:128] = wxp[0:64, :].T
    S["w_xprojp"] = wxpp
    wdt = np.asarray(inputs["L0_attn_dt_proj_w"], f32).T
    S["w_dtproj"] = np.ascontiguousarray(np.concatenate([wdt, wdt], axis=0))
    S["dt_bias"] = np.ascontiguousarray(np.asarray(inputs["L0_attn_dt_proj_b"], f32)[:, None])
    S["A_log"] = np.ascontiguousarray(np.asarray(inputs["L0_attn_A_log"], f32))
    S["D_param"] = np.ascontiguousarray(np.asarray(inputs["L0_attn_D_param"], f32)[:, None])
    S["w_outproj"] = np.ascontiguousarray(np.asarray(inputs["L0_attn_out_proj_w"], f32).T)

    S["wqkv"] = np.ascontiguousarray(np.asarray(inputs["L0_ffn_wqkv"], f32))
    S["wo"] = np.ascontiguousarray(np.asarray(inputs["L0_ffn_wo"], f32))
    S["ln_g"] = np.ascontiguousarray(np.asarray(inputs["L0_ffn_ln_g"], f32)[:, :, None])
    S["ln_b"] = np.ascontiguousarray(np.asarray(inputs["L0_ffn_ln_b"], f32)[:, :, None])
    w1 = np.asarray(inputs["L0_ffn_w1"], f32)
    b1 = np.asarray(inputs["L0_ffn_b1"], f32)
    w1p = np.zeros((2, DM, HB * 128), f32)
    b1p = np.zeros((2, HB * 128, 1), f32)
    for j in range(HB):
        lo, hi = j * 64, min((j + 1) * 64, HID)
        n = hi - lo
        w1p[:, :, j * 128:j * 128 + n] = w1[:, :, lo:hi]
        w1p[:, :, j * 128 + 64:j * 128 + 64 + n] = w1[:, :, HID + lo:HID + hi]
        b1p[:, j * 128:j * 128 + n, 0] = b1[:, lo:hi]
        b1p[:, j * 128 + 64:j * 128 + 64 + n, 0] = b1[:, HID + lo:HID + hi]
    S["w1p"] = w1p
    S["b1p"] = b1p
    w2 = np.asarray(inputs["L0_ffn_w2"], f32)
    w2p = np.zeros((2, HK * 128, DM), f32)
    for blk in range(HB):
        lo, hi = blk * 64, min((blk + 1) * 64, HID)
        n = hi - lo
        w2p[:, blk * 64:blk * 64 + n, :] = w2[:, lo:hi, :]
    S["w2p"] = w2p
    S["b2"] = np.ascontiguousarray(np.asarray(inputs["L0_ffn_b2"], f32)[:, :, None])
    qs = np.asarray(inputs["L0_ffn_q_scale"], f32)
    ks = np.asarray(inputs["L0_ffn_k_scale"], f32)
    roll = list(range(32, 64)) + list(range(32))
    S["qs_rep"] = np.ascontiguousarray(np.concatenate([qs, qs], 1)[:, :, None])
    S["ks_rep"] = np.ascontiguousarray(np.concatenate([ks, ks], 1)[:, :, None])
    S["qs_swap"] = np.ascontiguousarray(np.concatenate([qs[:, roll], qs[:, roll]], 1)[:, :, None])
    S["ks_swap"] = np.ascontiguousarray(np.concatenate([ks[:, roll], ks[:, roll]], 1)[:, :, None])

    half = np.arange(0, DH, 2, dtype=np.float64)
    inv_freq = 1.0 / (10000.0 ** (half / DH))
    t_ = np.arange(T, dtype=np.float64)
    fr = t_[:, None] * inv_freq[None, :]
    base = (half + 0.4 * DH) / (1.4 * DH)
    power = (t_ - T // 2) / float(WIN // 2)
    scl = base[None, :] ** power[:, None]
    cos, sin = np.cos(fr), np.sin(fr)

    def mk(a, swapneg):
        m = np.concatenate([a, -a] if swapneg else [a, a], axis=1).T
        return np.ascontiguousarray(np.concatenate([m, m], axis=0).astype(f32))

    S["T1q"] = mk(cos * scl, False)
    S["T2q"] = mk(sin * scl, True)
    S["T1k"] = mk(cos / scl, False)
    S["T2k"] = mk(sin / scl, True)
    S["identity"] = np.eye(128, dtype=f32).astype(BF)
    S["ones_col"] = np.ones((128, 1), f32)
    S["ones_row"] = np.ones((1, 128), f32)
    m = np.zeros((4, 128, WIN), f32)
    for j in range(4):
        i = np.arange(128)[:, None] + j * 128
        col = np.arange(WIN)[None, :]
        m[j] = np.where(col <= i, 0.0, NEG)
    S["cmask"] = np.ascontiguousarray(m.reshape(4 * 128, WIN))
    sel2 = np.zeros((128, 2), f32)
    sel2[:64, 0] = 1.0
    sel2[64:, 1] = 1.0
    S["sel2"] = sel2
    selT = np.zeros((2, 128), f32)
    selT[0, :64] = 1.0
    selT[1, 64:] = 1.0
    S["sel2T"] = selT
    per_core = [dict(S, x_fm=np.ascontiguousarray(x[b].T)) for b in range(8)]
    return per_core



def build(upto=None, taps=(), detect_races=True):
    nc = bass.Bass("TRN2", target_bir_lowering=False,
                   detect_race_conditions=detect_races)
    R = Rec()
    c = Ctx()
    c.nc, c.R = nc, R
    D = {}
    c.D = D
    c.scr_shapes = {}

    def din(name, shape, dt=F32):
        D[name] = nc.dram_tensor(name, list(shape), dt, kind="ExternalInput").ap()

    def dscr(name, shape, dt=F32):
        D[name] = nc.dram_tensor(name, list(shape), dt, kind="Internal").ap()
        c.scr_shapes[name] = (shape, dt)

    din("x_fm", [DM, T])
    din("w_inproj", [DM, 2 * DI])
    din("conv_w", [DI, 4])
    din("conv_b", [DI, 1])
    din("w_xprojp", [DI, 128])
    din("w_dtproj", [128, DI])
    din("dt_bias", [DI, 1])
    din("A_log", [DI, NST])
    din("D_param", [DI, 1])
    din("w_outproj", [DI, DM])
    din("wqkv", [2, DM, 3 * NH * DH])
    din("wo", [2, NH * DH, DM])
    din("ln_g", [2, DM, 1])
    din("ln_b", [2, DM, 1])
    din("w1p", [2, DM, HB * 128])
    din("b1p", [2, HB * 128, 1])
    din("w2p", [2, HK * 128, DM])
    din("b2", [2, DM, 1])
    din("qs_rep", [2, 128, 1])
    din("ks_rep", [2, 128, 1])
    din("qs_swap", [2, 128, 1])
    din("ks_swap", [2, 128, 1])
    din("T1q", [128, T])
    din("T2q", [128, T])
    din("T1k", [128, T])
    din("T2k", [128, T])
    din("identity", [128, 128], BF16)
    din("ones_col", [128, 1])
    din("ones_row", [1, 128])
    din("cmask", [4 * 128, WIN])
    din("sel2", [128, 2])
    din("sel2T", [2, 128])
    D["out"] = nc.dram_tensor("out", [DM, T], F32, kind="ExternalOutput").ap()

    dscr("xn_f", [DM, T])
    dscr("xn_b", [DM, 1 + T], BF16)
    dscr("xsil_f", [DI, T])
    dscr("xsil_b", [DI, T], BF16)
    dscr("zsil_f", [DI, T])
    dscr("Bscr", [NST, T])
    dscr("Cscr", [NST, T])
    dscr("y3_b", [DI, T], BF16)
    dscr("s_f", [DM, T])
    dscr("x1_f", [DM, 1 + T])
    dscr("x1_b", [DM, 1 + T], BF16)
    dscr("x2_f", [DM, 1 + T])
    dscr("x2_b", [DM, 1 + T], BF16)
    dscr("xh_b", [DM, T], BF16)
    dscr("h_scr", [HK * 128, T], BF16)
    import os as _os
    if _os.environ.get("DBG_ATTN"):
        dscr("dbg_o", [512, T], BF16)
        dscr("dbg_qp", [512, T], BF16)
        dscr("dbg_kp", [512, T], BF16)
        dscr("dbg_sm", [128, WIN])
        dscr("dbg_pb", [128, 2 * WIN], BF16)
        dscr("dbg_pt", [128, 8 * WIN], BF16)
        dscr("dbg_vt", [128, 2048], BF16)
        dscr("dbg_mx", [128, 8])

    tap_list = list(taps)
    for tp in tap_list:
        sh, dt = c.scr_shapes[tp]
        D["tap_" + tp] = nc.dram_tensor("tap_" + tp, list(sh), dt,
                                        kind="ExternalOutput").ap()

    ctx = contextlib.ExitStack()
    with ctx:
        SB = {}
        c.SB = SB

        def sb(name, shape, dt=F32):
            SB[name] = ctx.enter_context(nc.sbuf_tensor("sb_" + name, list(shape), dt))
            return SB[name]

        sb("ones", [128, 1])
        sb("onesr", [1, 128])
        sb("ident", [128, 128], BF16)
        sb("sel2", [128, 2])
        sb("sel2T", [2, 128])
        sb("cmask", [128, 4 * WIN])
        sb("A_sb", [128, 16 * NST])
        sb("convw", [128, 16 * 4])
        sb("convb", [128, 16])
        sb("dtb", [128, 16])
        sb("Dp", [128, 16])
        sb("lng", [128, 2 * 8])
        sb("lnb", [128, 2 * 8])
        sb("b1s", [128, 2 * HB])
        sb("b2s", [128, 2 * 8])
        sb("qsr", [128, 4])
        sb("qss", [128, 4])
        sb("row", [1, 6 * TCH])
        sb("fconst", [128, 2])
        sb("tiny", [128, 8])
        sb("dtlo", [128, T], BF16)
        sb("hst", [128, 16 * NST])

        PS = {}
        c.PS = PS
        for i in range(6):
            PS[i] = ctx.enter_context(nc.psum_tensor("psum%d" % i, [128, 512], F32))
        PS["t"] = ctx.enter_context(nc.psum_tensor("psumt", [128, 1024], BF16))

        sems = {}
        for sn in ("s_ld", "s_st", "s_pe", "s_v", "s_s", "s_g", "s_ld2", "s_st2"):
            sems[sn] = R.reg(ctx.enter_context(nc.semaphore(sn)))
        c.sems = sems

        def load(dst, src, eng="sync", sem=None):
            if sem is None:
                sem = sems["s_ld"] if eng == "sync" else sems["s_ld2"]
            return R.emit(eng, lambda e, dst=dst, src=src:
                          e.dma_start(out=dst, in_=src), inc=sem, inc_k=16)

        def store(dst, src, eng="sync"):
            sem = sems["s_st"] if eng == "sync" else sems["s_st2"]
            return R.emit(eng, lambda e, dst=dst, src=src:
                          e.dma_start(out=dst, in_=src), inc=sem, inc_k=16)

        def load_nc(dst, src, eng="sync"):
            sem = sems["s_ld"] if eng == "sync" else sems["s_ld2"]
            return R.emit(eng, lambda e, dst=dst, src=src:
                          e.dma_start(out=dst, in_=src,
                                      allow_slow_non_contiguous=True),
                          inc=sem, inc_k=16)

        def store_nc(dst, src, eng="sync"):
            sem = sems["s_st"] if eng == "sync" else sems["s_st2"]
            return R.emit(eng, lambda e, dst=dst, src=src:
                          e.dma_start(out=dst, in_=src,
                                      allow_slow_non_contiguous=True),
                          inc=sem, inc_k=16)
        c.load, c.store, c.store_nc, c.load_nc = load, store, store_nc, load_nc

        def barrier():
            R.wait("tensor", sems["s_ld"])
            R.emit("tensor", lambda e: e.matmul(PS[5][0:1, 0:1], SB["ones"][0:1, 0:1],
                                                SB["ones"][0:1, 0:1], start=True,
                                                stop=True), inc=sems["s_pe"])
            R.emit("vector", lambda e: e.engine_nop(), inc=sems["s_v"])
            R.emit("scalar", lambda e: e.memzero(SB["tiny"][:1, 0:1]),
                   inc=sems["s_s"])
            R.emit("gpsimd", lambda e: e.engine_nop(), inc=sems["s_g"])
            for eng in ENGS:
                for sn in ("s_pe", "s_v", "s_s", "s_g", "s_ld", "s_st", "s_ld2", "s_st2"):
                    R.wait(eng, sems[sn])
        c.barrier = barrier

        _init_consts(c)
        barrier()
        if upto == "0":
            for tp in tap_list:
                _emit_tap(c, tp)
                barrier()
            _replay(R, nc, ctx)
            return nc
        phases = [("A", _phase_ln), ("B", _phase_inproj), ("C", _phase_xproj),
                  ("D", _phase_scan), ("E", _phase_outproj), ("F", _phase_ffn),
                  ("G", _phase_final)]
        for name, fn in phases:
            fn(c)
            barrier()
            if upto == name:
                break
        for tp in tap_list:
            _emit_tap(c, tp)
            barrier()
        _replay(R, nc, ctx)
    return nc


def _replay(R, nc, ctx):
    block = ctx.enter_context(nc.Block())

    def mk(name):
        def f(e):
            for op in R.ops[name]:
                op(e)
        return f
    block.sync(mk("sync"))
    block.tensor(mk("tensor"))
    block.vector(mk("vector"))
    block.scalar(mk("scalar"))
    block.gpsimd(mk("gpsimd"))


def _emit_tap(c, tp):
    R = c.R
    sh, dt = c.scr_shapes[tp]
    rows, cols = sh
    src, dst = c.D[tp], c.D["tap_" + tp]
    with c.nc.sbuf_tensor("tap_" + tp + "_buf", [128, 4096], dt) as buf:
        for r0 in range(0, rows, 128):
            rr = min(128, rows - r0)
            for c0 in range(0, cols, 4096):
                cc = min(4096, cols - c0)
                tl = c.load_nc(buf[:rr, :cc], src[r0:r0 + rr, c0:c0 + cc])
                R.wait("sync", c.sems["s_ld"], tl)
                c.store_nc(dst[r0:r0 + rr, c0:c0 + cc], buf[:rr, :cc])
                R.wait("sync", c.sems["s_st"])


def _init_consts(c):
    R, D, SB, load = c.R, c.D, c.SB, c.load
    s_ld, s_s, s_v = c.sems["s_ld"], c.sems["s_s"], c.sems["s_v"]
    load(SB["ones"][:], D["ones_col"])
    load(SB["onesr"][:], D["ones_row"])
    load(SB["ident"][:], D["identity"])
    load(SB["sel2"][:], D["sel2"])
    load(SB["sel2T"][:], D["sel2T"])
    for j in range(4):
        load(SB["cmask"][:, j * WIN:(j + 1) * WIN], D["cmask"][j * 128:(j + 1) * 128, :])
    for dt_ in range(16):
        load(SB["convw"][:, dt_ * 4:(dt_ + 1) * 4], D["conv_w"][dt_ * 128:(dt_ + 1) * 128, :])
        load(SB["convb"][:, dt_:dt_ + 1], D["conv_b"][dt_ * 128:(dt_ + 1) * 128, :])
        load(SB["dtb"][:, dt_:dt_ + 1], D["dt_bias"][dt_ * 128:(dt_ + 1) * 128, :])
        load(SB["Dp"][:, dt_:dt_ + 1], D["D_param"][dt_ * 128:(dt_ + 1) * 128, :])
    for l in range(2):
        for j in range(8):
            load(SB["lng"][:, l * 8 + j:l * 8 + j + 1], D["ln_g"][l, j * 128:(j + 1) * 128, :])
            load(SB["lnb"][:, l * 8 + j:l * 8 + j + 1], D["ln_b"][l, j * 128:(j + 1) * 128, :])
            load(SB["b2s"][:, l * 8 + j:l * 8 + j + 1], D["b2"][l, j * 128:(j + 1) * 128, :])
        for j in range(HB):
            load(SB["b1s"][:, l * HB + j:l * HB + j + 1],
                 D["b1p"][l, j * 128:(j + 1) * 128, :])
        load(SB["qsr"][:, 2 * l:2 * l + 1], D["qs_rep"][l])
        load(SB["qsr"][:, 2 * l + 1:2 * l + 2], D["ks_rep"][l])
        load(SB["qss"][:, 2 * l:2 * l + 1], D["qs_swap"][l])
        load(SB["qss"][:, 2 * l + 1:2 * l + 2], D["ks_swap"][l])
    with c.nc.sbuf_tensor("initbuf", [128, 1024], F32) as ib:
        tl = None
        for dt_ in range(16):
            tl = load(ib[:, dt_ * NST:(dt_ + 1) * NST],
                      D["A_log"][dt_ * 128:(dt_ + 1) * 128, :])
        R.wait("scalar", s_ld, tl)
        R.emit("scalar", lambda e, ib=ib: e.activation(SB["A_sb"][:], ib[:, 0:16 * NST],
                                                       AF.Exp), inc=s_s)
        R.wait("vector", s_s)
        R.emit("vector", lambda e: e.tensor_scalar(SB["A_sb"][:], SB["A_sb"][:], -1.0,
                                                   None, ALU.mult), inc=s_v)
        R.emit("vector", lambda e: e.memset(SB["fconst"][:, 0:1], EPS), inc=s_v)
        R.emit("vector", lambda e: e.memset(SB["fconst"][:, 1:2], 1.0), inc=s_v)
        R.emit("vector", lambda e, ib=ib: e.memset(ib[:], 0.0), inc=s_v)
        R.emit("vector", lambda e: e.memset(SB["hst"][:], 0.0), inc=s_v)
        R.wait("sync", s_v)
        ibf = ib[:].bitcast(BF16)
        for name in ("xn_b", "x1_b", "x2_b"):
            for r0 in range(0, DM, 128):
                c.store_nc(D[name][r0:r0 + 128, 0:1], ibf[:, 0:1])
        for name in ("x1_f", "x2_f"):
            for r0 in range(0, DM, 128):
                c.store_nc(D[name][r0:r0 + 128, 0:1], ib[:, 0:1])
        for c0 in range(0, T, 2 * TCH):
            c.store(D["h_scr"][HPAD:HPAD + 64, c0:c0 + 2 * TCH],
                    ibf[:64, 0:2 * TCH])
        R.wait("sync", c.sems["s_st"])


def _ln_stats_rows(c, xs_tiles, sq_tiles, ps_sum, ps_sq, eps=EPS, dm=DM):
    R, SB = c.R, c.SB
    s_pe, s_v, s_s = c.sems["s_pe"], c.sems["s_v"], c.sems["s_s"]
    rw = SB["row"]
    n = len(xs_tiles)
    R.wait("tensor", s_v)
    R.wait("tensor", s_s)
    for j, ap in enumerate(xs_tiles):
        R.emit("tensor", lambda e, ap=ap, j=j, n=n: e.matmul(
            ps_sum[0:1, :], SB["ones"][:], ap, start=(j == 0), stop=(j == n - 1)),
            inc=(s_pe if j == n - 1 else None))
    for j, ap in enumerate(sq_tiles):
        R.emit("tensor", lambda e, ap=ap, j=j, n=n: e.matmul(
            ps_sq[0:1, :], SB["ones"][:], ap, start=(j == 0), stop=(j == n - 1)),
            inc=(s_pe if j == n - 1 else None))
    R.wait("vector", s_pe)
    R.emit("vector", lambda e: e.tensor_scalar(rw[:, 0:TCH], ps_sum[0:1, :], 1.0 / dm,
                                               None, ALU.mult), inc=s_v)
    R.emit("vector", lambda e: e.tensor_scalar(rw[:, TCH:2 * TCH], ps_sq[0:1, :], 1.0 / dm,
                                               None, ALU.mult), inc=s_v)
    R.wait("vector", s_v)
    R.emit("vector", lambda e: e.tensor_tensor(rw[:, 2 * TCH:3 * TCH], rw[:, 0:TCH],
                                               rw[:, 0:TCH], ALU.mult), inc=s_v)
    R.wait("vector", s_v)
    R.emit("vector", lambda e: e.tensor_tensor(rw[:, TCH:2 * TCH], rw[:, TCH:2 * TCH],
                                               rw[:, 2 * TCH:3 * TCH], ALU.subtract), inc=s_v)
    R.wait("scalar", s_v)
    R.emit("scalar", lambda e: e.activation(rw[:, 2 * TCH:3 * TCH], rw[:, TCH:2 * TCH],
                                            AF.Ln, bias=SB["fconst"][0:1, 0:1]), inc=s_s)
    R.wait("scalar", s_s)
    R.emit("scalar", lambda e: e.activation(rw[:, 2 * TCH:3 * TCH], rw[:, 2 * TCH:3 * TCH],
                                            AF.Exp, scale=-0.5), inc=s_s)
    R.wait("vector", s_s)
    R.emit("vector", lambda e: e.tensor_tensor(rw[:, 3 * TCH:4 * TCH], rw[:, 0:TCH],
                                               rw[:, 2 * TCH:3 * TCH], ALU.mult), inc=s_v)
    R.wait("tensor", s_v)


def _bcast_row(c, src_row, dst, psum, n):
    R, SB = c.R, c.SB
    R.wait("tensor", c.sems["s_s"])
    R.emit("tensor", lambda e, src_row=src_row, psum=psum, n=n: e.matmul(
        psum[:, 0:n], SB["onesr"][:], src_row, start=True, stop=True), inc=c.sems["s_pe"])
    R.wait("scalar", c.sems["s_pe"])
    return R.emit("scalar", lambda e, dst=dst, psum=psum, n=n: e.activation(
        dst, psum[:, 0:n], AF.Copy), inc=c.sems["s_s"])


def _apply_ln(c, Xt, Sq, Bc, nt=8):
    R = c.R
    s_v = c.sems["s_v"]
    for j in range(nt):
        R.emit("vector", lambda e, j=j: e.tensor_tensor(
            Sq[:, j * TCH:(j + 1) * TCH], Xt[:, j * TCH:(j + 1) * TCH],
            Bc[:, 0:TCH], ALU.mult), inc=s_v)
    R.wait("vector", s_v)
    for j in range(nt):
        R.emit("vector", lambda e, j=j: e.tensor_tensor(
            Sq[:, j * TCH:(j + 1) * TCH], Sq[:, j * TCH:(j + 1) * TCH],
            Bc[:, TCH:2 * TCH], ALU.subtract), inc=s_v)
    R.wait("vector", s_v)


def _phase_ln(c):
    R, D, SB, PS = c.R, c.D, c.SB, c.PS
    s_ld, s_st, s_v, s_s = (c.sems[k] for k in ("s_ld", "s_st", "s_v", "s_s"))
    with c.nc.sbuf_tensor("lnX", [128, 8 * TCH], F32) as Xt, \
         c.nc.sbuf_tensor("lnS", [128, 8 * TCH], F32) as Sq, \
         c.nc.sbuf_tensor("lnB", [128, 2 * TCH], F32) as Bc, \
         c.nc.sbuf_tensor("lnO", [128, 8 * TCH], BF16) as Ob:
        for ci in range(NCH):
            cw = slice(ci * TCH, (ci + 1) * TCH)
            R.wait("sync", s_v)
            for j in range(8):
                c.load(Xt[:, j * TCH:(j + 1) * TCH], D["x_fm"][j * 128:(j + 1) * 128, cw])
            R.wait("scalar", s_ld)
            for j in range(8):
                R.emit("scalar", lambda e, j=j: e.activation(
                    Sq[:, j * TCH:(j + 1) * TCH], Xt[:, j * TCH:(j + 1) * TCH],
                    AF.Square), inc=s_s)
            R.wait("tensor", s_s)
            xs = [Xt[:, j * TCH:(j + 1) * TCH] for j in range(8)]
            sq = [Sq[:, j * TCH:(j + 1) * TCH] for j in range(8)]
            _ln_stats_rows(c, xs, sq, PS[0], PS[1])
            _bcast_row(c, SB["row"][:, 2 * TCH:3 * TCH], Bc[:, 0:TCH], PS[2], TCH)
            sbc = _bcast_row(c, SB["row"][:, 3 * TCH:4 * TCH], Bc[:, TCH:2 * TCH], PS[3], TCH)
            R.wait("vector", s_s, sbc)
            _apply_ln(c, Xt, Sq, Bc)
            R.emit("vector", lambda e: e.tensor_copy(Ob[:], Sq[:]), inc=s_v)
            R.wait("sync", s_v)
            for j in range(8):
                c.store(D["xn_f"][j * 128:(j + 1) * 128, cw], Sq[:, j * TCH:(j + 1) * TCH])
                c.store(D["xn_b"][j * 128:(j + 1) * 128, 1 + ci * TCH:1 + (ci + 1) * TCH],
                        Ob[:, j * TCH:(j + 1) * TCH])
            R.wait("sync", s_st)


def _cast_weight(c, wsrc_ap, wdst_bf, stage_f32):
    R = c.R
    s_ld, s_v = c.sems["s_ld"], c.sems["s_v"]
    shp = wsrc_ap.shape
    n = 1
    for s in shp[1:]:
        n *= s
    tl = c.load(stage_f32[:shp[0], :n], wsrc_ap)
    R.wait("vector", s_ld, tl)
    R.emit("vector", lambda e, wdst_bf=wdst_bf, stage_f32=stage_f32, shp=shp, n=n:
           e.tensor_copy(wdst_bf, stage_f32[:shp[0], :n]), inc=s_v)
    R.wait("sync", s_v)


def _phase_inproj(c):
    R, D, SB, PS = c.R, c.D, c.SB, c.PS
    s_ld, s_st, s_pe, s_v, s_s = (c.sems[k] for k in ("s_ld", "s_st", "s_pe", "s_v", "s_s"))
    with c.nc.sbuf_tensor("ipXN", [128, 8 * T], BF16) as XNB, \
         c.nc.sbuf_tensor("ipWS", [128, 128], F32) as WS, \
         c.nc.sbuf_tensor("ipWB", [128, 8 * 128], BF16) as WB, \
         c.nc.sbuf_tensor("ipXC", [128, 4 + TCH], F32) as XC, \
         c.nc.sbuf_tensor("ipAC", [128, TCH], F32) as AC, \
         c.nc.sbuf_tensor("ipXS", [128, TCH], F32) as XS, \
         c.nc.sbuf_tensor("ipXSB", [128, TCH], BF16) as XSB:
        for j in range(8):
            c.load(XNB[:, j * T:(j + 1) * T], D["xn_b"][j * 128:(j + 1) * 128, 1:1 + T])
        for m in range(32):
            R.wait("sync", s_pe)
            for k in range(8):
                _cast_weight(c, D["w_inproj"][k * 128:(k + 1) * 128, m * 128:(m + 1) * 128],
                             WB[:, k * 128:(k + 1) * 128], WS)
            R.wait("tensor", s_v)
            R.wait("tensor", s_ld)
            for ci in range(NCH):
                pb = PS[ci % 2]
                cw = slice(ci * TCH, (ci + 1) * TCH)
                R.wait("tensor", s_s)
                for k in range(8):
                    R.emit("tensor", lambda e, k=k, ci=ci, pb=pb: e.matmul(
                        pb[:], WB[:, k * 128:(k + 1) * 128],
                        XNB[:, k * T + ci * TCH:k * T + (ci + 1) * TCH],
                        start=(k == 0), stop=(k == 7)), inc=(s_pe if k == 7 else None))
                R.wait("scalar", s_pe)
                R.wait("scalar", s_st)
                if m >= 16:
                    R.emit("scalar", lambda e, pb=pb: e.activation(XS[:], pb[:], AF.Silu),
                           inc=s_s)
                    R.wait("sync", s_s)
                    c.store(D["zsil_f"][(m - 16) * 128:(m - 15) * 128, cw], XS[:])
                else:
                    if ci == 0:
                        R.emit("vector", lambda e: e.memset(XC[:, 0:4], 0.0), inc=s_v)
                    R.wait("scalar", s_v)
                    R.emit("scalar", lambda e, pb=pb: e.activation(XC[:, 4:4 + TCH], pb[:],
                                                                   AF.Copy), inc=s_s)
                    R.wait("vector", s_s)
                    R.emit("vector", lambda e, m=m: e.tensor_scalar(
                        AC[:], XC[:, 1:1 + TCH], SB["convw"][:, 4 * m:4 * m + 1], None,
                        ALU.mult), inc=s_v)
                    R.wait("vector", s_v)
                    for kk in range(1, 4):
                        R.emit("vector", lambda e, m=m, kk=kk: e.scalar_tensor_tensor(
                            AC[:], XC[:, 1 + kk:1 + kk + TCH],
                            SB["convw"][:, 4 * m + kk:4 * m + kk + 1], AC[:],
                            ALU.mult, ALU.add), inc=s_v)
                        R.wait("vector", s_v)
                    R.emit("vector", lambda e: e.tensor_copy(XC[:, 0:4], XC[:, TCH:TCH + 4]),
                           inc=s_v)
                    R.wait("scalar", s_v)
                    R.emit("scalar", lambda e, m=m: e.activation(
                        XS[:], AC[:], AF.Silu, bias=SB["convb"][:, m:m + 1]), inc=s_s)
                    R.wait("vector", s_s)
                    R.wait("vector", s_st)
                    R.emit("vector", lambda e: e.tensor_copy(XSB[:], XS[:]), inc=s_v)
                    R.wait("sync", s_v)
                    R.wait("sync", s_s)
                    c.store(D["xsil_f"][m * 128:(m + 1) * 128, cw], XS[:])
                    c.store(D["xsil_b"][m * 128:(m + 1) * 128, cw], XSB[:])


def _phase_xproj(c):
    R, D, SB, PS = c.R, c.D, c.SB, c.PS
    s_ld, s_st, s_pe, s_v, s_s = (c.sems[k] for k in ("s_ld", "s_st", "s_pe", "s_v", "s_s"))
    with c.nc.sbuf_tensor("xpWS", [128, 128], F32) as WS, \
         c.nc.sbuf_tensor("xpWB", [128, 16 * 128], BF16) as WB, \
         c.nc.sbuf_tensor("xpXT", [128, 16 * TCH], BF16) as XT, \
         c.nc.sbuf_tensor("xpBC", [48, TCH], F32) as BC:
        for k in range(16):
            _cast_weight(c, D["w_xprojp"][k * 128:(k + 1) * 128, :],
                         WB[:, k * 128:(k + 1) * 128], WS)
        R.wait("tensor", s_v)
        for ci in range(NCH):
            cw = slice(ci * TCH, (ci + 1) * TCH)
            for k in range(16):
                c.load(XT[:, k * TCH:(k + 1) * TCH], D["xsil_b"][k * 128:(k + 1) * 128, cw])
            R.wait("tensor", s_ld)
            R.wait("tensor", s_s)
            for k in range(16):
                R.emit("tensor", lambda e, k=k: e.matmul(
                    PS[0][:], WB[:, k * 128:(k + 1) * 128], XT[:, k * TCH:(k + 1) * TCH],
                    start=(k == 0), stop=(k == 15)), inc=(s_pe if k == 15 else None))
            R.wait("scalar", s_pe)
            R.wait("scalar", s_st)
            R.emit("scalar", lambda e, cw=cw: e.activation(
                SB["dtlo"][0:64, cw], PS[0][64:128, :], AF.Copy), inc=s_s)
            R.emit("scalar", lambda e, cw=cw: e.activation(
                SB["dtlo"][64:128, cw], PS[0][64:128, :], AF.Copy), inc=s_s)
            R.emit("scalar", lambda e: e.activation(BC[0:16, :], PS[0][0:16, :], AF.Copy),
                   inc=s_s)
            R.emit("scalar", lambda e: e.activation(BC[32:48, :], PS[0][32:48, :], AF.Copy),
                   inc=s_s)
            R.wait("sync", s_s)
            c.store(D["Bscr"][:, cw], BC[0:16, :])
            c.store(D["Cscr"][:, cw], BC[32:48, :])


def _phase_scan(c):
    R, D, SB, PS = c.R, c.D, c.SB, c.PS
    s_ld, s_st, s_pe, s_v, s_s, s_g = (c.sems[k] for k in
                                       ("s_ld", "s_st", "s_pe", "s_v", "s_s", "s_g"))
    with c.nc.sbuf_tensor("scWS", [128, 512], F32) as WS, \
         c.nc.sbuf_tensor("scWB", [128, DI], BF16) as WB, \
         c.nc.sbuf_tensor("scBB", [128, NST * TCH], F32) as BB, \
         c.nc.sbuf_tensor("scCB", [128, NST * TCH], F32) as CB, \
         c.nc.sbuf_tensor("scHB", [128, NST * TCH], F32) as HBuf, \
         c.nc.sbuf_tensor("scDA", [128, 8 * TCH], F32) as DAr, \
         c.nc.sbuf_tensor("scDB", [128, 8 * TCH], F32) as DBr, \
         c.nc.sbuf_tensor("scDT", [128, TCH], F32) as DT, \
         c.nc.sbuf_tensor("scSP", [128, TCH], F32) as SP, \
         c.nc.sbuf_tensor("scU", [128, TCH], F32) as U, \
         c.nc.sbuf_tensor("scXS", [128, TCH], F32) as XSf, \
         c.nc.sbuf_tensor("scZS", [128, TCH], F32) as ZS, \
         c.nc.sbuf_tensor("scY3", [128, TCH], BF16) as Y3:
        for p in range(4):
            _cast_weight(c, D["w_dtproj"][:, p * 512:(p + 1) * 512],
                         WB[:, p * 512:(p + 1) * 512], WS)
        R.wait("tensor", s_v)
        for ci in range(NCH):
            cw = slice(ci * TCH, (ci + 1) * TCH)
            R.wait("sync", s_v)
            R.wait("sync", s_g)
            c.load(BB[:].rearrange("p (a b) -> p a b", a=NST),
                   D["Bscr"][:, cw].unsqueeze(0).partition_broadcast(128))
            lc = c.load(CB[:].rearrange("p (a b) -> p a b", a=NST),
                        D["Cscr"][:, cw].unsqueeze(0).partition_broadcast(128))
            for dt_ in range(16):
                pb = PS[dt_ % 2]
                R.wait("tensor", s_s)
                if dt_ % 2 == 0:
                    R.emit("tensor", lambda e, dt_=dt_, pb=pb, cw=cw: e.matmul(
                        pb[:], WB[0:64, dt_ * 128:(dt_ + 1) * 128],
                        SB["dtlo"][0:64, cw], start=True, stop=True,
                        tile_position=(0, 0)), inc=s_pe)
                else:
                    R.emit("tensor", lambda e, dt_=dt_, pb=pb, cw=cw: e.matmul(
                        pb[:], WB[64:128, dt_ * 128:(dt_ + 1) * 128],
                        SB["dtlo"][64:128, cw], start=True, stop=True,
                        tile_position=(64, 0)), inc=s_pe)
                R.wait("scalar", s_pe)
                R.wait("scalar", s_v)
                R.emit("scalar", lambda e, dt_=dt_, pb=pb: e.activation(
                    DT[:], pb[:], AF.Relu, bias=SB["dtb"][:, dt_:dt_ + 1]), inc=s_s)
                R.emit("scalar", lambda e, dt_=dt_, pb=pb: e.activation(
                    SP[:], pb[:], AF.Abs, bias=SB["dtb"][:, dt_:dt_ + 1]), inc=s_s)
                R.wait("scalar", s_s)
                R.emit("scalar", lambda e: e.activation(SP[:], SP[:], AF.Exp, scale=-1.0),
                       inc=s_s)
                R.wait("scalar", s_s)
                R.emit("scalar", lambda e: e.activation(SP[:], SP[:], AF.Ln,
                                                        bias=SB["fconst"][:, 1:2]),
                       inc=s_s)
                R.wait("vector", s_s)
                R.wait("vector", s_g)
                R.emit("vector", lambda e: e.tensor_tensor(DT[:], DT[:], SP[:], ALU.add),
                       inc=s_v)
                R.wait("sync", s_v)
                lx = c.load(XSf[:], D["xsil_f"][dt_ * 128:(dt_ + 1) * 128, cw])
                R.wait("vector", s_ld, lx)
                R.wait("vector", s_v)
                R.emit("vector", lambda e: e.tensor_tensor(U[:], DT[:], XSf[:], ALU.mult),
                       inc=s_v)
                u_done = R.val(s_v)
                R.wait("scalar", s_v, u_done)
                R.wait("gpsimd", s_v, u_done)
                R.wait("gpsimd", c.sems["s_ld"], lc)
                scan_tgts = {}
                for n in range(NST):
                    slot = n % 8
                    if n >= 8:
                        R.wait("scalar", s_v, scan_tgts[n - 8])
                        R.wait("gpsimd", s_v, scan_tgts[n - 8])
                    R.emit("scalar", lambda e, dt_=dt_, n=n, slot=slot: e.activation(
                        DAr[:, slot * TCH:(slot + 1) * TCH], DT[:], AF.Exp,
                        scale=SB["A_sb"][:, dt_ * NST + n:dt_ * NST + n + 1]), inc=s_s)
                    R.emit("gpsimd", lambda e, n=n, slot=slot: e.tensor_tensor(
                        DBr[:, slot * TCH:(slot + 1) * TCH], U[:],
                        BB[:, n * TCH:(n + 1) * TCH], ALU.mult), inc=s_g)
                    R.wait("vector", s_s)
                    R.wait("vector", s_g)
                    init = SB["hst"][:, dt_ * NST + n:dt_ * NST + n + 1] if ci > 0 else 0.0
                    scan_tgts[n] = R.emit(
                        "vector", lambda e, n=n, slot=slot, init=init: e.tensor_tensor_scan(
                            HBuf[:, n * TCH:(n + 1) * TCH],
                            DAr[:, slot * TCH:(slot + 1) * TCH],
                            DBr[:, slot * TCH:(slot + 1) * TCH],
                            init, ALU.mult, ALU.add), inc=s_v)
                R.wait("vector", s_v)
                R.emit("vector", lambda e, dt_=dt_: e.tensor_copy(
                    SB["hst"][:, dt_ * NST:(dt_ + 1) * NST],
                    HBuf[:].rearrange("p (a b) -> p a b", a=NST)[:, :, TCH - 1:TCH]
                    .squeeze(2)), inc=s_v)
                R.wait("vector", s_v)
                R.emit("vector", lambda e: e.tensor_tensor(HBuf[:], HBuf[:], CB[:],
                                                           ALU.mult), inc=s_v)
                R.wait("vector", s_v)
                for lvl in (8, 4, 2, 1):
                    R.emit("vector", lambda e, lvl=lvl: e.tensor_tensor(
                        HBuf[:, 0:lvl * TCH], HBuf[:, 0:lvl * TCH],
                        HBuf[:, lvl * TCH:2 * lvl * TCH], ALU.add), inc=s_v)
                    R.wait("vector", s_v)
                R.wait("sync", s_v)
                lz = c.load(ZS[:], D["zsil_f"][dt_ * 128:(dt_ + 1) * 128, cw])
                R.emit("vector", lambda e, dt_=dt_: e.scalar_tensor_tensor(
                    HBuf[:, 0:TCH], XSf[:], SB["Dp"][:, dt_:dt_ + 1], HBuf[:, 0:TCH],
                    ALU.mult, ALU.add), inc=s_v)
                R.wait("vector", c.sems["s_ld"], lz)
                R.wait("vector", s_v)
                R.wait("vector", s_st)
                R.emit("vector", lambda e: e.tensor_tensor(Y3[:], HBuf[:, 0:TCH], ZS[:],
                                                           ALU.mult), inc=s_v)
                R.wait("sync", s_v)
                c.store(D["y3_b"][dt_ * 128:(dt_ + 1) * 128, cw], Y3[:])


def _phase_outproj(c):
    R, D, SB, PS = c.R, c.D, c.SB, c.PS
    s_ld, s_st, s_pe, s_v, s_s = (c.sems[k] for k in ("s_ld", "s_st", "s_pe", "s_v", "s_s"))
    with c.nc.sbuf_tensor("opWS", [128, 512], F32) as WS, \
         c.nc.sbuf_tensor("opWB", [128, 16 * DM], BF16) as WB, \
         c.nc.sbuf_tensor("opYT", [128, 16 * TCH], BF16) as YT, \
         c.nc.sbuf_tensor("opSO", [128, 2 * TCH], F32) as SO:
        for k in range(16):
            for p in range(2):
                _cast_weight(c, D["w_outproj"][k * 128:(k + 1) * 128, p * 512:(p + 1) * 512],
                             WB[:, k * DM + p * 512:k * DM + (p + 1) * 512], WS)
        R.wait("tensor", s_v)
        for ci in range(NCH):
            cw = slice(ci * TCH, (ci + 1) * TCH)
            R.wait("sync", s_pe)
            for k in range(16):
                c.load(YT[:, k * TCH:(k + 1) * TCH], D["y3_b"][k * 128:(k + 1) * 128, cw])
            R.wait("tensor", s_ld)
            for m in range(8):
                pb = PS[m % 2]
                R.wait("tensor", s_s)
                for k in range(16):
                    R.emit("tensor", lambda e, k=k, m=m, pb=pb: e.matmul(
                        pb[:], WB[:, k * DM + m * 128:k * DM + (m + 1) * 128],
                        YT[:, k * TCH:(k + 1) * TCH],
                        start=(k == 0), stop=(k == 15)), inc=(s_pe if k == 15 else None))
                R.wait("scalar", s_pe)
                R.wait("scalar", s_st)
                R.emit("scalar", lambda e, m=m, pb=pb: e.activation(
                    SO[:, (m % 2) * TCH:(m % 2 + 1) * TCH], pb[:], AF.Copy), inc=s_s)
                R.wait("sync", s_s)
                c.store(D["s_f"][m * 128:(m + 1) * 128, cw],
                        SO[:, (m % 2) * TCH:(m % 2 + 1) * TCH])


def _rotary_side(c, QFt, QBC, CS1, CS2, OUTB, HSC):
    R = c.R
    s_v = c.sems["s_v"]
    for half in (0, 1):
        r0 = half * 64
        a = HSC[:, 0:WIN]
        b = HSC[:, WIN:2 * WIN]
        cc = HSC[:, 2 * WIN:3 * WIN]
        R.wait("vector", s_v)
        R.emit("vector", lambda e, r0=r0, a=a, QFt=QFt, QBC=QBC: e.tensor_tensor(
            a, QFt[r0:r0 + 64, :], QBC[r0:r0 + 64, :], ALU.mult), inc=s_v)
        R.wait("vector", s_v)
        R.emit("vector", lambda e, a=a, b=b, CS1=CS1: e.tensor_tensor(
            b, a, CS1[0:64, :], ALU.mult), inc=s_v)
        R.emit("vector", lambda e, a=a, cc=cc, CS2=CS2: e.tensor_tensor(
            cc[32:64, :], a[0:32, :], CS2[0:32, :], ALU.mult), inc=s_v)
        R.emit("vector", lambda e, a=a, cc=cc, CS2=CS2: e.tensor_tensor(
            cc[0:32, :], a[32:64, :], CS2[32:64, :], ALU.mult), inc=s_v)
        R.wait("vector", s_v)
        R.emit("vector", lambda e, b=b, cc=cc, r0=r0, OUTB=OUTB: e.tensor_tensor(
            OUTB[r0:r0 + 64, :], b, cc, ALU.add), inc=s_v)
    R.wait("vector", s_v)


def _attn_body(c, w, hp, QP, KP, VT, PB, PT, SM, MX, OB, par, l):
    R, SB, PS = c.R, c.SB, c.PS
    s_pe, s_v, s_s = c.sems["s_pe"], c.sems["s_v"], c.sems["s_s"]
    prev = 1 - par
    kc0 = 0 if w > 0 else 4
    for tj in range(4):
        R.wait("tensor", s_s)
        R.wait("tensor", s_v)
        for half in (0, 1):
            r0 = half * 64
            tp = (0, 0) if half == 0 else (64, 0)
            bprev = PS[0] if half == 0 else PS[2]
            bself = PS[1] if half == 0 else PS[3]
            qblk = QP[r0:r0 + 64, hp * WIN + tj * 128:hp * WIN + tj * 128 + 128]
            if w > 0:
                R.emit("tensor", lambda e, r0=r0, tp=tp, bprev=bprev, hp=hp, prev=prev,
                       qblk=qblk: e.matmul(
                    bprev[:], qblk,
                    KP[r0:r0 + 64, prev * 4 * WIN + hp * WIN:prev * 4 * WIN + (hp + 1) * WIN],
                    start=True, stop=True, tile_position=tp), inc=None)
            R.emit("tensor", lambda e, r0=r0, tp=tp, bself=bself, hp=hp, par=par,
                   qblk=qblk: e.matmul(
                bself[:], qblk,
                KP[r0:r0 + 64, par * 4 * WIN + hp * WIN:par * 4 * WIN + (hp + 1) * WIN],
                start=True, stop=True, tile_position=tp),
                inc=(s_pe if half == 1 else None))
        R.wait("vector", s_pe)
        for half in (0, 1):
            bprev = PS[0] if half == 0 else PS[2]
            bself = PS[1] if half == 0 else PS[3]
            head = 2 * hp + half
            R.wait("vector", s_s)
            R.emit("vector", lambda e, bself=bself, tj=tj: e.tensor_tensor(
                SM[:], bself[:], SB["cmask"][:, tj * WIN:(tj + 1) * WIN], ALU.add),
                inc=s_v)
            R.wait("vector", s_v)
            R.emit("vector", lambda e: e.tensor_reduce(MX[:, 0:1], SM[:], AX.X, ALU.max),
                   inc=s_v)
            if w > 0:
                R.emit("vector", lambda e, bprev=bprev: e.tensor_reduce(
                    MX[:, 1:2], bprev[:], AX.X, ALU.max), inc=s_v)
                R.wait("vector", s_v)
                R.emit("vector", lambda e: e.tensor_tensor(MX[:, 0:1], MX[:, 0:1],
                                                           MX[:, 1:2], ALU.max), inc=s_v)
            R.wait("vector", s_v)
            R.emit("vector", lambda e: e.tensor_scalar(MX[:, 2:3], MX[:, 0:1], -QKS, None,
                                                       ALU.mult), inc=s_v)
            R.wait("scalar", s_v)
            if w > 0:
                R.emit("scalar", lambda e, bprev=bprev: e.activation(
                    PB[:, 0:WIN], bprev[:], AF.Exp, bias=MX[:, 2:3], scale=QKS,
                    accum_out=MX[:, 3:4]), inc=s_s)
            R.emit("scalar", lambda e: e.activation(
                PB[:, WIN:2 * WIN], SM[:], AF.Exp, bias=MX[:, 2:3], scale=QKS,
                accum_out=MX[:, 4:5]), inc=s_s)
            R.wait("vector", s_s)
            if w > 0:
                R.emit("vector", lambda e: e.tensor_tensor(MX[:, 3:4], MX[:, 3:4],
                                                           MX[:, 4:5], ALU.add), inc=s_v)
            else:
                R.emit("vector", lambda e: e.tensor_copy(MX[:, 3:4], MX[:, 4:5]), inc=s_v)
            R.wait("vector", s_v)
            R.emit("vector", lambda e: e.reciprocal(MX[:, 5:6], MX[:, 3:4]), inc=s_v)
            R.wait("vector", s_v)
            lo = 0 if w > 0 else WIN
            R.emit("vector", lambda e, lo=lo: e.tensor_scalar(
                PB[:, lo:2 * WIN], PB[:, lo:2 * WIN], MX[:, 5:6], None, ALU.mult), inc=s_v)
            R.wait("tensor", s_v)
            import os as _os
            if _os.environ.get("DBG_ATTN") and l == 0 and w == 1 and hp == 0 \
                    and tj == 0 and half == 0:
                R.wait("sync", s_v)
                c.store(c.D["dbg_sm"][:, :], SM[:])
                c.store(c.D["dbg_pb"][:, :], PB[:])
                c.store(c.D["dbg_mx"][:, 0:6], MX[:, 0:6])
            for g0 in range(kc0, 8, 4):
                R.wait("tensor", s_s)
                kend = min(g0 + 4, 8) - 1
                for kc in range(g0, kend + 1):
                    R.emit("tensor", lambda e, kc=kc: e.transpose(
                        PS["t"][:, (kc % 4) * 128:(kc % 4 + 1) * 128],
                        PB[:, kc * 128:(kc + 1) * 128], SB["ident"][:]),
                        inc=(s_pe if kc == kend else None))
                R.wait("scalar", s_pe)
                for kc in range(g0, min(g0 + 4, 8)):
                    R.emit("scalar", lambda e, kc=kc, tj=tj, half=half: e.activation(
                        PT[:, (half * 8 + kc) * WIN + tj * 128:
                           (half * 8 + kc) * WIN + (tj + 1) * 128],
                        PS["t"][:, (kc % 4) * 128:(kc % 4 + 1) * 128], AF.Copy), inc=s_s)
    import os as _os
    if _os.environ.get("DBG_ATTN") and l == 0 and w == 1 and hp == 0:
        R.wait("sync", s_s)
        c.store(c.D["dbg_pt"][:, :], PT[:, 0:8 * WIN])
        c.store(c.D["dbg_vt"][:, :], VT[:, (1 - par) * 2048:(1 - par) * 2048 + 2048])
    R.wait("tensor", s_s)
    R.wait("tensor", s_v)
    for kc in range(kc0, 8):
        vt_src = prev if kc < 4 else par
        vchunk = kc % 4
        for half in (0, 1):
            tp = (0, 0) if half == 0 else (0, 64)
            h = 2 * hp + half
            oslice = PS[4][0:64, :] if half == 0 else PS[5][64:128, :]
            R.emit("tensor", lambda e, kc=kc, tp=tp, h=h, vt_src=vt_src, vchunk=vchunk,
                   oslice=oslice, half=half, kc0=kc0: e.matmul(
                oslice,
                VT[:, vt_src * 2048 + (h * 4 + vchunk) * 64:
                   vt_src * 2048 + (h * 4 + vchunk) * 64 + 64],
                PT[:, (half * 8 + kc) * WIN:(half * 8 + kc + 1) * WIN],
                start=(kc == kc0), stop=(kc == 7),
                tile_position=tp), inc=(s_pe if (kc == 7 and half == 1) else None))
    R.wait("scalar", s_pe)
    R.emit("scalar", lambda e, hp=hp: e.activation(
        OB[0:64, hp * WIN:(hp + 1) * WIN], PS[4][0:64, :], AF.Copy), inc=s_s)
    R.emit("scalar", lambda e, hp=hp: e.activation(
        OB[64:128, hp * WIN:(hp + 1) * WIN], PS[5][64:128, :], AF.Copy), inc=s_s)
    R.wait("tensor", s_s)


def _phase_ffn(c):
    import os
    nlayers = int(os.environ.get("FFN_LAYERS", "2"))
    stages = os.environ.get("FFN_STAGES", "a2hw")
    for l in range(nlayers):
        if "a" in stages:
            _ffn_attn_layer(c, l)
        if "2" in stages:
            _ffn_ln2(c, l)
        if "h" in stages:
            _ffn_w1(c, l)
        if "w" in stages:
            _ffn_w2(c, l)


def _ffn_attn_layer(c, l):
    R, D, SB, PS = c.R, c.D, c.SB, c.PS
    s_ld, s_st, s_pe, s_v, s_s = (c.sems[k] for k in ("s_ld", "s_st", "s_pe", "s_v", "s_s"))
    nc = c.nc
    import os
    if True:
        xin_b = D["xn_b"] if l == 0 else D["x2_b"]
        xin_f = D["xn_f"] if l == 0 else D["x2_f"]
        xin_off = 0 if l == 0 else 1
        with contextlib.ExitStack() as fst:
            en = fst.enter_context
            WS = en(nc.sbuf_tensor("atWS%d" % l, [128, 256], F32))
            WQ = en(nc.sbuf_tensor("atWQ%d" % l, [128, 8 * 1536], BF16))
            WO = en(nc.sbuf_tensor("atWO%d" % l, [128, 4 * DM], BF16))
            XI = en(nc.sbuf_tensor("atXI%d" % l, [128, 8 * (WIN + 1)], BF16))
            QF = en(nc.sbuf_tensor("atQF%d" % l, [128, 4 * WIN], F32))
            KF = en(nc.sbuf_tensor("atKF%d" % l, [128, 4 * WIN], F32))
            QP = en(nc.sbuf_tensor("atQP%d" % l, [128, 4 * WIN], BF16))
            KP = en(nc.sbuf_tensor("atKP%d" % l, [128, 2 * 4 * WIN], BF16))
            VT = en(nc.sbuf_tensor("atVT%d" % l, [128, 2 * 2048], BF16))
            VS = en(nc.sbuf_tensor("atVS%d" % l, [64, 2 * WIN], BF16))
            NR = en(nc.sbuf_tensor("atNR%d" % l, [2, WIN], F32))
            BCt = en(nc.sbuf_tensor("atBC%d" % l, [128, WIN], F32))
            CS = en(nc.sbuf_tensor("atCS%d" % l, [128, 4 * WIN], F32))
            HSC = en(nc.sbuf_tensor("atHS%d" % l, [64, 3 * WIN], F32))
            SM = en(nc.sbuf_tensor("atSM%d" % l, [128, WIN], F32))
            MX = en(nc.sbuf_tensor("atMX%d" % l, [128, 8], F32))
            PB = en(nc.sbuf_tensor("atPB%d" % l, [128, 2 * WIN], BF16))
            PT = en(nc.sbuf_tensor("atPT%d" % l, [128, 16 * WIN], BF16))
            OB = en(nc.sbuf_tensor("atOB%d" % l, [128, 4 * WIN], BF16))
            SQt = en(nc.sbuf_tensor("atSQ%d" % l, [128, WIN], F32))
            XM = en(nc.sbuf_tensor("atXM%d" % l, [128, WIN], F32))
            XMB = en(nc.sbuf_tensor("atXMB%d" % l, [128, WIN], BF16))
            for mi in range(12):
                for k in range(8):
                    _cast_weight(c, D["wqkv"][l, k * 128:(k + 1) * 128,
                                              mi * 128:(mi + 1) * 128],
                                 WQ[:, k * 1536 + mi * 128:k * 1536 + (mi + 1) * 128], WS)
            for mi in range(8):
                for k in range(4):
                    _cast_weight(c, D["wo"][l, k * 128:(k + 1) * 128,
                                            mi * 128:(mi + 1) * 128],
                                 WO[:, k * DM + mi * 128:k * DM + (mi + 1) * 128], WS)
            R.wait("tensor", s_v)
            nwin = int(os.environ.get("FFN_WINDOWS", "8"))
            for w in range(nwin):
                wt = w * WIN
                par = w % 2
                R.wait("sync", s_pe)
                R.wait("sync", s_v)
                for j in range(8):
                    c.load(XI[:, j * (WIN + 1):(j + 1) * (WIN + 1)],
                           xin_b[j * 128:(j + 1) * 128, wt:wt + WIN + 1])
                c.load(CS[:, 0:WIN], D["T1q"][:, wt:wt + WIN], sem=c.sems["s_ld2"])
                c.load(CS[:, WIN:2 * WIN], D["T2q"][:, wt:wt + WIN], sem=c.sems["s_ld2"])
                c.load(CS[:, 2 * WIN:3 * WIN], D["T1k"][:, wt:wt + WIN], sem=c.sems["s_ld2"])
                lt = c.load(CS[:, 3 * WIN:4 * WIN], D["T2k"][:, wt:wt + WIN],
                            sem=c.sems["s_ld2"])
                R.wait("vector", c.sems["s_ld2"], lt)
                R.emit("vector", lambda e, l=l: e.tensor_scalar(
                    CS[:, 0:WIN], CS[:, 0:WIN], SB["qsr"][:, 2 * l:2 * l + 1], None,
                    ALU.mult), inc=s_v)
                R.emit("vector", lambda e, l=l: e.tensor_scalar(
                    CS[:, WIN:2 * WIN], CS[:, WIN:2 * WIN],
                    SB["qsr"][:, 2 * l:2 * l + 1], None, ALU.mult), inc=s_v)
                R.emit("vector", lambda e, l=l: e.tensor_scalar(
                    CS[:, 2 * WIN:3 * WIN], CS[:, 2 * WIN:3 * WIN],
                    SB["qsr"][:, 2 * l + 1:2 * l + 2], None, ALU.mult), inc=s_v)
                R.emit("vector", lambda e, l=l: e.tensor_scalar(
                    CS[:, 3 * WIN:4 * WIN], CS[:, 3 * WIN:4 * WIN],
                    SB["qsr"][:, 2 * l + 1:2 * l + 2], None, ALU.mult), inc=s_v)
                if os.environ.get("DBG_BUILD"):
                    print("l%dw%d: s_ld tgt=%d s_pe=%d s_v=%d s_s=%d" % (
                        l, w, R.val(s_ld), R.val(s_pe), R.val(s_v), R.val(s_s)))
                R.wait("tensor", s_ld)
                parts = os.environ.get("ATTN_PARTS", "qrbo")
                for mi in range(int(os.environ.get("ATTN_NMI", "12")) if "q" in parts else 0):
                    pb = PS[4 + (mi % 2)]
                    R.wait("tensor", s_s)
                    for k in range(8):
                        off = 1 if k < 4 else 0
                        R.emit("tensor", lambda e, mi=mi, k=k, off=off, pb=pb: e.matmul(
                            pb[:], WQ[:, k * 1536 + mi * 128:k * 1536 + (mi + 1) * 128],
                            XI[:, k * (WIN + 1) + off:k * (WIN + 1) + off + WIN],
                            start=(k == 0), stop=(k == 7)), inc=(s_pe if k == 7 else None))
                    R.wait("scalar", s_pe)
                    if mi < 4:
                        R.emit("scalar", lambda e, mi=mi, pb=pb: e.activation(
                            QF[:, mi * WIN:(mi + 1) * WIN], pb[:], AF.Copy), inc=s_s)
                    elif mi < 8:
                        R.emit("scalar", lambda e, mi=mi, pb=pb: e.activation(
                            KF[:, (mi - 4) * WIN:(mi - 3) * WIN], pb[:], AF.Copy), inc=s_s)
                    else:
                        R.wait("scalar", s_pe)
                        R.emit("scalar", lambda e, pb=pb: e.activation(
                            VS[:, 0:WIN], pb[0:64, :], AF.Copy), inc=s_s)
                        R.emit("scalar", lambda e, pb=pb: e.activation(
                            VS[:, WIN:2 * WIN], pb[64:128, :], AF.Copy), inc=s_s)
                        R.wait("tensor", s_s)
                        hh = (mi - 8) * 2
                        for tk in range(4):
                            R.emit("tensor", lambda e, tk=tk: e.transpose(
                                PS["t"][:, tk * 64:(tk + 1) * 64],
                                VS[:, tk * 128:(tk + 1) * 128],
                                SB["ident"][0:64, 0:64]), inc=None)
                            R.emit("tensor", lambda e, tk=tk: e.transpose(
                                PS["t"][:, 256 + tk * 64:256 + (tk + 1) * 64],
                                VS[:, WIN + tk * 128:WIN + (tk + 1) * 128],
                                SB["ident"][0:64, 0:64]),
                                inc=(s_pe if tk == 3 else None))
                        R.wait("scalar", s_pe)
                        for tk in range(4):
                            R.emit("scalar", lambda e, tk=tk, hh=hh, par=par: e.activation(
                                VT[:, par * 2048 + (hh * 4 + tk) * 64:
                                   par * 2048 + (hh * 4 + tk) * 64 + 64],
                                PS["t"][:, tk * 64:(tk + 1) * 64], AF.Copy), inc=s_s)
                            R.emit("scalar", lambda e, tk=tk, hh=hh, par=par: e.activation(
                                VT[:, par * 2048 + ((hh + 1) * 4 + tk) * 64:
                                   par * 2048 + ((hh + 1) * 4 + tk) * 64 + 64],
                                PS["t"][:, 256 + tk * 64:256 + (tk + 1) * 64], AF.Copy),
                                inc=s_s)
                        R.wait("tensor", s_s)
                for side, SRC in ((("q", QF), ("k", KF)) if "r" in parts else ()):
                    for tj in range(4):
                        R.wait("scalar", s_v)
                        R.emit("scalar", lambda e, SRC=SRC, tj=tj: e.activation(
                            SQt[:], SRC[:, tj * WIN:(tj + 1) * WIN], AF.Square), inc=s_s)
                        R.wait("tensor", s_s)
                        R.emit("tensor", lambda e: e.matmul(
                            PS[0][0:2, :], SB["sel2"][:], SQt[:], start=True, stop=True),
                            inc=s_pe)
                        R.wait("scalar", s_pe)
                        R.emit("scalar", lambda e: e.activation(NR[:], PS[0][0:2, :],
                                                                AF.Sqrt), inc=s_s)
                        R.wait("vector", s_s)
                        R.emit("vector", lambda e: e.tensor_scalar(
                            NR[:], NR[:], 1e-12, None, ALU.max), inc=s_v)
                        R.wait("vector", s_v)
                        R.emit("vector", lambda e: e.reciprocal(NR[:], NR[:]), inc=s_v)
                        R.wait("tensor", s_v)
                        R.emit("tensor", lambda e: e.matmul(
                            PS[1][:], SB["sel2T"][:], NR[:], start=True, stop=True),
                            inc=s_pe)
                        R.wait("scalar", s_pe)
                        R.wait("scalar", s_v)
                        R.emit("scalar", lambda e: e.activation(BCt[:], PS[1][:], AF.Copy),
                               inc=s_s)
                        R.wait("vector", s_s)
                        if side == "q":
                            _rotary_side(c, QF[:, tj * WIN:(tj + 1) * WIN], BCt,
                                         CS[:, 0:WIN], CS[:, WIN:2 * WIN],
                                         QP[:, tj * WIN:(tj + 1) * WIN], HSC)
                        else:
                            _rotary_side(c, KF[:, tj * WIN:(tj + 1) * WIN], BCt,
                                         CS[:, 2 * WIN:3 * WIN], CS[:, 3 * WIN:4 * WIN],
                                         KP[:, par * 4 * WIN + tj * WIN:
                                            par * 4 * WIN + (tj + 1) * WIN], HSC)
                import os as _os
                if _os.environ.get("DBG_ATTN") and l == 0:
                    R.wait("sync", s_v)
                    for tj in range(4):
                        c.store(D["dbg_qp"][tj * 128:(tj + 1) * 128, wt:wt + WIN],
                                QP[:, tj * WIN:(tj + 1) * WIN])
                        c.store(D["dbg_kp"][tj * 128:(tj + 1) * 128, wt:wt + WIN],
                                KP[:, par * 4 * WIN + tj * WIN:
                                   par * 4 * WIN + (tj + 1) * WIN])
                for hp in range(4 if "b" in parts else 0):
                    _attn_body(c, w, hp, QP, KP, VT, PB, PT, SM, MX, OB, par, l)
                if _os.environ.get("DBG_ATTN") and l == 0:
                    R.wait("sync", s_s)
                    for hp in range(4):
                        c.store(D["dbg_o"][hp * 128:(hp + 1) * 128, wt:wt + WIN],
                                OB[:, hp * WIN:(hp + 1) * WIN])
                for mi in range(8 if "o" in parts else 0):
                    pb = PS[4 + (mi % 2)]
                    R.wait("tensor", s_s)
                    R.wait("tensor", s_v)
                    for k in range(4):
                        R.emit("tensor", lambda e, mi=mi, k=k, pb=pb: e.matmul(
                            pb[:], WO[:, k * DM + mi * 128:k * DM + (mi + 1) * 128],
                            OB[:, k * WIN:(k + 1) * WIN], start=(k == 0), stop=(k == 3)),
                            inc=(s_pe if k == 3 else None))
                    R.wait("sync", s_v)
                    lxf = c.load(XM[:], xin_f[mi * 128:(mi + 1) * 128,
                                              xin_off + wt:xin_off + wt + WIN])
                    R.wait("vector", s_pe)
                    R.wait("vector", c.sems["s_ld"], lxf)
                    R.wait("vector", s_st)
                    R.emit("vector", lambda e, pb=pb: e.tensor_tensor(XM[:], pb[:], XM[:],
                                                                      ALU.add), inc=s_v)
                    R.wait("vector", s_v)
                    R.emit("vector", lambda e: e.tensor_copy(XMB[:], XM[:]), inc=s_v)
                    R.wait("sync", s_v)
                    c.store(D["x1_f"][mi * 128:(mi + 1) * 128, 1 + wt:1 + wt + WIN], XM[:])
                    c.store(D["x1_b"][mi * 128:(mi + 1) * 128, 1 + wt:1 + wt + WIN], XMB[:])


def _ffn_ln2(c, l):
    R, D, SB, PS = c.R, c.D, c.SB, c.PS
    s_ld, s_st, s_v, s_s = (c.sems[k] for k in ("s_ld", "s_st", "s_v", "s_s"))
    nc = c.nc
    with nc.sbuf_tensor("l2X%d" % l, [128, 8 * TCH], F32) as Xt, \
         nc.sbuf_tensor("l2S%d" % l, [128, 8 * TCH], F32) as Sq, \
         nc.sbuf_tensor("l2B%d" % l, [128, 2 * TCH], F32) as Bc, \
         nc.sbuf_tensor("l2O%d" % l, [128, TCH], BF16) as Ob:
        R.wait("sync", s_st)
        for ci in range(NCH):
            R.wait("sync", s_v)
            for j in range(8):
                off = 1 if j < 4 else 0
                c.load(Xt[:, j * TCH:(j + 1) * TCH],
                       D["x1_f"][j * 128:(j + 1) * 128, off + ci * TCH:off + (ci + 1) * TCH])
            R.wait("scalar", s_ld)
            for j in range(8):
                R.emit("scalar", lambda e, j=j: e.activation(
                    Sq[:, j * TCH:(j + 1) * TCH], Xt[:, j * TCH:(j + 1) * TCH],
                    AF.Square), inc=s_s)
            R.wait("tensor", s_s)
            xs = [Xt[:, j * TCH:(j + 1) * TCH] for j in range(8)]
            sq = [Sq[:, j * TCH:(j + 1) * TCH] for j in range(8)]
            _ln_stats_rows(c, xs, sq, PS[0], PS[1])
            _bcast_row(c, SB["row"][:, 2 * TCH:3 * TCH], Bc[:, 0:TCH], PS[2], TCH)
            sbc = _bcast_row(c, SB["row"][:, 3 * TCH:4 * TCH], Bc[:, TCH:2 * TCH], PS[3], TCH)
            R.wait("vector", s_s, sbc)
            _apply_ln(c, Xt, Sq, Bc)
            for j in range(8):
                R.wait("vector", s_st)
                R.emit("vector", lambda e, j=j, l=l: e.tensor_scalar(
                    Ob[:], Sq[:, j * TCH:(j + 1) * TCH],
                    SB["lng"][:, l * 8 + j:l * 8 + j + 1],
                    SB["lnb"][:, l * 8 + j:l * 8 + j + 1], ALU.mult, ALU.add), inc=s_v)
                R.wait("sync", s_v)
                c.store(D["xh_b"][j * 128:(j + 1) * 128, ci * TCH:(ci + 1) * TCH], Ob[:])


def _ffn_w1(c, l):
    R, D, SB, PS = c.R, c.D, c.SB, c.PS
    s_ld, s_st, s_pe, s_v, s_s = (c.sems[k] for k in ("s_ld", "s_st", "s_pe", "s_v", "s_s"))
    nc = c.nc
    HT2 = T // 2
    with nc.sbuf_tensor("w1WS%d" % l, [128, 128], F32) as WS, \
         nc.sbuf_tensor("w1WB%d" % l, [128, 8 * 128], BF16) as WB, \
         nc.sbuf_tensor("w1XH%d" % l, [128, 8 * HT2], BF16) as XH, \
         nc.sbuf_tensor("w1GB%d" % l, [64, TCH], BF16) as GB, \
         nc.sbuf_tensor("w1HB%d" % l, [64, TCH], BF16) as HBo:
        R.wait("sync", s_st)
        for g in range(2):
            R.wait("sync", s_pe)
            for j in range(8):
                c.load(XH[:, j * HT2:(j + 1) * HT2],
                       D["xh_b"][j * 128:(j + 1) * 128, g * HT2:(g + 1) * HT2])
            for m in range(HB):
                R.wait("sync", s_pe)
                for k in range(8):
                    _cast_weight(c, D["w1p"][l, k * 128:(k + 1) * 128,
                                             m * 128:(m + 1) * 128],
                                 WB[:, k * 128:(k + 1) * 128], WS)
                R.wait("tensor", s_v)
                R.wait("tensor", s_ld)
                for ci in range(NCH // 2):
                    cig = g * (NCH // 2) + ci
                    pb = PS[ci % 2]
                    R.wait("tensor", s_s)
                    R.wait("tensor", s_v)
                    for k in range(8):
                        R.emit("tensor", lambda e, k=k, ci=ci, pb=pb: e.matmul(
                            pb[:], WB[:, k * 128:(k + 1) * 128],
                            XH[:, k * HT2 + ci * TCH:k * HT2 + (ci + 1) * TCH],
                            start=(k == 0), stop=(k == 7)), inc=(s_pe if k == 7 else None))
                    R.wait("scalar", s_pe)
                    R.wait("scalar", s_v)
                    R.emit("scalar", lambda e, m=m, l=l, pb=pb: e.activation(
                        GB[:], pb[64:128, :], AF.Gelu,
                        bias=SB["b1s"][64:128, l * HB + m:l * HB + m + 1]), inc=s_s)
                    R.wait("vector", s_s)
                    R.wait("vector", s_st)
                    R.emit("vector", lambda e, m=m, l=l, pb=pb: e.scalar_tensor_tensor(
                        HBo[:], pb[0:64, :], SB["b1s"][0:64, l * HB + m:l * HB + m + 1],
                        GB[:], ALU.add, ALU.mult), inc=s_v)
                    R.wait("sync", s_v)
                    c.store(D["h_scr"][m * 64:(m + 1) * 64, cig * TCH:(cig + 1) * TCH],
                            HBo[:])


def _ffn_w2(c, l):
    R, D, SB, PS = c.R, c.D, c.SB, c.PS
    s_ld, s_st, s_pe, s_v, s_s = (c.sems[k] for k in ("s_ld", "s_st", "s_pe", "s_v", "s_s"))
    nc = c.nc
    with nc.sbuf_tensor("w2WS%d" % l, [128, 128], F32) as WS, \
         nc.sbuf_tensor("w2WB%d" % l, [128, HK * 8 * 128], BF16) as WB, \
         nc.sbuf_tensor("w2HT%d" % l, [128, HK * TCH], BF16) as HT, \
         nc.sbuf_tensor("w2XO%d" % l, [128, TCH], F32) as XO, \
         nc.sbuf_tensor("w2XOB%d" % l, [128, TCH], BF16) as XOB, \
         nc.sbuf_tensor("w2XR%d" % l, [128, TCH], F32) as XR:
        for m in range(8):
            for k in range(HK):
                _cast_weight(c, D["w2p"][l, k * 128:(k + 1) * 128, m * 128:(m + 1) * 128],
                             WB[:, (m * HK + k) * 128:(m * HK + k + 1) * 128], WS)
        R.wait("tensor", s_v)
        R.wait("sync", s_st)
        for ci in range(NCH):
            cw = slice(ci * TCH, (ci + 1) * TCH)
            R.wait("sync", s_pe)
            for k in range(HK):
                c.load(HT[:, k * TCH:(k + 1) * TCH], D["h_scr"][k * 128:(k + 1) * 128, cw])
            R.wait("tensor", s_ld)
            for m in range(8):
                pb = PS[m % 2]
                R.wait("tensor", s_v)
                for k in range(HK):
                    R.emit("tensor", lambda e, m=m, k=k, pb=pb: e.matmul(
                        pb[:], WB[:, (m * HK + k) * 128:(m * HK + k + 1) * 128],
                        HT[:, k * TCH:(k + 1) * TCH],
                        start=(k == 0), stop=(k == HK - 1)),
                        inc=(s_pe if k == HK - 1 else None))
                R.wait("sync", s_v)
                lxr = c.load(XR[:], D["x1_f"][m * 128:(m + 1) * 128,
                                              1 + ci * TCH:1 + (ci + 1) * TCH])
                R.wait("vector", s_pe)
                R.wait("vector", c.sems["s_ld"], lxr)
                R.wait("vector", s_st)
                R.emit("vector", lambda e, m=m, l=l, pb=pb: e.scalar_tensor_tensor(
                    XO[:], pb[:], SB["b2s"][:, l * 8 + m:l * 8 + m + 1], XR[:],
                    ALU.add, ALU.add), inc=s_v)
                R.wait("vector", s_v)
                R.emit("vector", lambda e: e.tensor_copy(XOB[:], XO[:]), inc=s_v)
                R.wait("sync", s_v)
                c.store(D["x2_f"][m * 128:(m + 1) * 128, 1 + ci * TCH:1 + (ci + 1) * TCH],
                        XO[:])
                c.store(D["x2_b"][m * 128:(m + 1) * 128, 1 + ci * TCH:1 + (ci + 1) * TCH],
                        XOB[:])


def _phase_final(c):
    R, D = c.R, c.D
    s_ld, s_st, s_v, s_s = (c.sems[k] for k in ("s_ld", "s_st", "s_v", "s_s"))
    nc = c.nc
    with nc.sbuf_tensor("fnX", [128, TCH], F32) as X, \
         nc.sbuf_tensor("fnS", [128, TCH], F32) as Sv, \
         nc.sbuf_tensor("fnG", [128, TCH], F32) as G, \
         nc.sbuf_tensor("fnO", [128, TCH], F32) as O:
        for j in range(8):
            for ci in range(NCH):
                cw = slice(ci * TCH, (ci + 1) * TCH)
                R.wait("sync", s_v)
                R.wait("sync", s_s)
                c.load(X[:], D["x_fm"][j * 128:(j + 1) * 128, cw])
                c.load(Sv[:], D["s_f"][j * 128:(j + 1) * 128, cw])
                lg = c.load(G[:], D["x2_f"][j * 128:(j + 1) * 128,
                                            1 + ci * TCH:1 + (ci + 1) * TCH])
                R.wait("scalar", s_ld, lg)
                R.emit("scalar", lambda e: e.activation(G[:], G[:], AF.Silu), inc=s_s)
                R.wait("vector", s_s)
                R.wait("vector", s_st)
                R.emit("vector", lambda e: e.tensor_tensor(O[:], Sv[:], G[:], ALU.mult),
                       inc=s_v)
                R.wait("vector", s_v)
                R.emit("vector", lambda e: e.tensor_tensor(O[:], O[:], X[:], ALU.add),
                       inc=s_v)
                R.wait("sync", s_v)
                c.store(D["out"][j * 128:(j + 1) * 128, cw], O[:])


def kernel(**inputs):
    per_core = host_prep(inputs)
    nc = build()
    res = run_bass_kernel_spmd(nc, per_core, core_ids=list(range(8)))
    out = np.stack([np.ascontiguousarray(r["out"].T) for r in res.results], axis=0)
    return out
